# version 11
# baseline (speedup 1.0000x reference)
import contextlib
import numpy as np
import ml_dtypes
import concourse.bass as bass
import concourse.mybir as mybir
from concourse.bass_utils import run_bass_kernel_spmd

F32 = mybir.dt.float32
BF16 = mybir.dt.bfloat16
I16 = mybir.dt.int16
U8 = mybir.dt.uint8
AF = mybir.ActivationFunctionType
ALU = mybir.AluOpType
AX = mybir.AxisListType
bf = ml_dtypes.bfloat16

B, S, E = 4, 2048, 4096
TOK = 1024
ECH = E // 128
HQ, HKV, D = 32, 8, 128
GQ = HQ // HKV
II = 11008
ICH = II // 128
NSB = II // 256
EPS = 1e-5
ISC = float(1.0 / np.sqrt(128.0))
NEG = -30000.0
IB_D = 11
DOWN_BLOCKS = []
_s = 0
while _s < ICH:
    DOWN_BLOCKS.append(min(IB_D, ICH - _s))
    _s += IB_D
KB = 1024

ENGS = ("sync", "tensor", "vector", "scalar", "gpsimd")


class Plan:

    def __init__(self, sems):
        self.ops = {e: [] for e in ENGS}
        self.sems = sems
        self.cnt = {k: 0 for k in sems}

    def wait(self, eng, sem, val):
        if val <= 0:
            return
        self.ops[eng].append(lambda E, s=self.sems[sem], v=val: E.wait_ge(s, v))

    def emit(self, eng, fn):
        self.ops[eng].append(fn)

    def inc(self, eng, fn, sem, n):
        self.cnt[sem] += n
        self.ops[eng].append(
            lambda E, f=fn, s=self.sems[sem], m=n: f(E).then_inc(s, m))
        return self.cnt[sem]

    def replay(self, block):
        for eng in ENGS:
            lst = self.ops[eng]
            if not lst:
                continue

            def section(E, lst=lst):
                for fn in lst:
                    fn(E)
            getattr(block, eng)(section)


def build_nc(taps=(), stop_phase=None, dbg_inter=False, paranoid=0, dbg_parts=False,
             dbg_pd0=False):
    nc = bass.Bass("TRN2", target_bir_lowering=False)

    x_own = nc.dram_tensor("x_own", [TOK, E], F32, kind="ExternalInput").ap()
    x_oth = nc.dram_tensor("x_oth", [TOK, E], F32, kind="ExternalInput").ap()
    wq = nc.dram_tensor("wq", [E, E], F32, kind="ExternalInput").ap()
    wk = nc.dram_tensor("wk", [E, HKV * D], F32, kind="ExternalInput").ap()
    wv = nc.dram_tensor("wv", [E, HKV * D], F32, kind="ExternalInput").ap()
    wo = nc.dram_tensor("wo", [HQ * D, E], F32, kind="ExternalInput").ap()
    g16 = nc.dram_tensor("g16", [E, II], I16, kind="ExternalInput").ap()
    u16 = nc.dram_tensor("u16", [E, II], I16, kind="ExternalInput").ap()
    d16 = nc.dram_tensor("d16", [II, E], I16, kind="ExternalInput").ap()
    gs_d = nc.dram_tensor("gs", [128, ICH], F32, kind="ExternalInput").ap()
    us_d = nc.dram_tensor("us", [128, ICH], F32, kind="ExternalInput").ap()
    dsb_d = nc.dram_tensor("dsb", [128, E], F32, kind="ExternalInput").ap()
    cosq_d = nc.dram_tensor("cosq", [128, 512], BF16, kind="ExternalInput").ap()
    sinq_d = nc.dram_tensor("sinq", [128, 512], BF16, kind="ExternalInput").ap()
    cosk_d = nc.dram_tensor("cosk", [128, 512], BF16, kind="ExternalInput").ap()
    sink_d = nc.dram_tensor("sink", [128, 512], BF16, kind="ExternalInput").ap()
    mask_d = nc.dram_tensor("mask", [128, 128], BF16, kind="ExternalInput").ap()
    ident_d = nc.dram_tensor("ident", [128, 128], BF16, kind="ExternalInput").ap()
    out_d = nc.dram_tensor("out", [TOK, E], F32, kind="ExternalOutput").ap()

    wq_scr = nc.dram_tensor("wq_scr", [E, E], BF16, kind="Internal").ap()
    wk_scr = nc.dram_tensor("wk_scr", [E, HKV * D], BF16, kind="Internal").ap()
    wv_scr = nc.dram_tensor("wv_scr", [E, HKV * D], BF16, kind="Internal").ap()
    wo_scr = nc.dram_tensor("wo_scr", [HQ * D, E], BF16, kind="Internal").ap()
    inter_scr = nc.dram_tensor("inter_scr", [II, TOK], BF16,
                               kind="ExternalOutput" if dbg_inter else "Internal").ap()
    parts_d = (nc.dram_tensor("parts", [8 * TOK, E], F32, kind="ExternalOutput").ap()
               if dbg_parts else None)
    pd0_d = (nc.dram_tensor("pd0", [11 * 128, E + TOK], BF16, kind="ExternalOutput").ap()
             if dbg_pd0 else None)

    tap_shapes = {"xn_own": [128, ECH * TOK], "xn_oth": [128, ECH * TOK],
                  "k_fm": [128, HKV * TOK], "v_tok": [128, 8 * HKV * 128],
                  "attn_fm": [128, HQ * TOK], "q_fm": [128, 2 * TOK]}
    tap_d = {t: nc.dram_tensor("tap_" + t, tap_shapes[t], BF16,
                               kind="ExternalOutput").ap() for t in taps}

    st = contextlib.ExitStack()
    with st:
        cosq = st.enter_context(nc.sbuf_tensor("s_cosq", [128, 512], BF16))
        sinq = st.enter_context(nc.sbuf_tensor("s_sinq", [128, 512], BF16))
        cosk = st.enter_context(nc.sbuf_tensor("s_cosk", [128, 512], BF16))
        sink = st.enter_context(nc.sbuf_tensor("s_sink", [128, 512], BF16))
        mask = st.enter_context(nc.sbuf_tensor("s_mask", [128, 128], BF16))
        ident = st.enter_context(nc.sbuf_tensor("s_ident", [128, 128], BF16))
        gs = st.enter_context(nc.sbuf_tensor("s_gs", [128, ICH], F32))
        us = st.enter_context(nc.sbuf_tensor("s_us", [128, ICH], F32))
        stats = st.enter_context(nc.sbuf_tensor("stats", [128, 32], F32))
        q_pair = st.enter_context(nc.sbuf_tensor("q_pair", [128, 2048], BF16))
        arena = st.enter_context(nc.sbuf_tensor("arena", [128, 192 * KB], U8))
        psA = st.enter_context(nc.psum_tensor("psA", [128, 1024], F32))
        psB = st.enter_context(nc.psum_tensor("psB", [128, 2048], BF16))
        psC = st.enter_context(nc.psum_tensor("psC", [128, 2048], BF16))
        psD = st.enter_context(nc.psum_tensor("psD", [128, 512], F32))
        psT = st.enter_context(nc.psum_tensor("psT", [128, 1024], BF16))
        sem_h = {}
        for s in ("sLoad", "sPrep", "sGp", "sPE", "sDVE", "sACT",
                  "sPk", "sPv", "sPq", "sPo", "sG", "sU", "sD0", "sD1",
                  "sX0", "sX1", "sIl", "sR0", "sR1", "sR2"):
            sem_h[s] = st.enter_context(nc.semaphore(s))

        psTf = psT[:].bitcast(F32)
        psBf = psB[:].bitcast(F32)
        psCf = psC[:].bitcast(F32)
        PS4 = [psA[:, 0:256], psA[:, 512:768], psBf[:, 0:256], psBf[:, 512:768]]

        def AR(off_kb, n_kb, dt):
            v = arena[:, off_kb * KB:(off_kb + n_kb) * KB]
            return v.bitcast(dt) if dt != U8 else v

        xn_own = AR(0, 64, BF16)
        bigx = AR(64, 64, BF16)
        k_fm = AR(128, 16, BF16)
        v_tok = AR(144, 16, BF16)
        STG = 160

        P = Plan(sem_h)
        phase_marks = {}

        def barrier():
            for e in ENGS:
                for s in sem_h:
                    P.wait(e, s, P.cnt[s])

        def mark(name):
            phase_marks[name] = ({e: len(P.ops[e]) for e in ENGS}, dict(P.cnt))

        def xn_tile(xn, e_c, t_c):
            return xn[:, e_c * TOK + t_c * 128: e_c * TOK + (t_c + 1) * 128]

        def xn_half(xn, e_c, tg):
            return xn[:, e_c * TOK + tg * 512: e_c * TOK + (tg + 1) * 512]

        for dst, src in [(cosq, cosq_d), (sinq, sinq_d), (cosk, cosk_d),
                         (sink, sink_d), (mask, mask_d), (ident, ident_d),
                         (gs, gs_d), (us, us_d)]:
            P.inc("sync", lambda E, d=dst, s=src: E.dma_start(out=d[:], in_=s),
                  "sLoad", 16)
        consts_ld = P.cnt["sLoad"]

        prep = {}
        prep_sem = {"wk": "sPk", "wv": "sPv", "wq": "sPq", "wo": "sPo"}
        for name, dst, src, rows in [("wk", wk_scr, wk, 4096), ("wv", wv_scr, wv, 4096),
                                     ("wq", wq_scr, wq, 4096), ("wo", wo_scr, wo, 4096)]:
            for r0 in range(0, rows, 256):
                P.inc("gpsimd", lambda E, d=dst, s=src, r=r0: E.dma_start(
                    out=d[r:r + 256, :], in_=s[r:r + 256, :]), prep_sem[name], 16)
            prep[name] = P.cnt[prep_sem[name]]

        def emit_ln(xdram, xn_dst, entry_pe=0):
            xs = [AR(STG, 8, F32), AR(STG + 8, 8, F32)]
            tmp = AR(STG + 16, 8, F32)
            xt = AR(STG + 24, 8, BF16)
            last_cons = 0
            last_norm = 0
            last_sqrt = 0
            P.wait("sync", "sPE", entry_pe)
            for t_c in range(8):
                P.wait("sync", "sDVE", last_norm)
                for half in range(2):
                    P.inc("sync", lambda E, h=half, t=t_c: E.dma_start(
                        out=xs[h],
                        in_=xdram[t * 128:(t + 1) * 128, h * 2048:(h + 1) * 2048]),
                        "sLoad", 16)
                ld = P.cnt["sLoad"]

                def gv(fn):
                    t = P.inc("vector", fn, "sDVE", 1)
                    P.wait("vector", "sDVE", t)
                    return t

                P.wait("vector", "sLoad", ld)
                P.wait("vector", "sACT", last_sqrt)
                P.wait("vector", "sPE", last_cons)
                gv(lambda E: E.reduce_sum(stats[:, 0:1], xs[0], axis=AX.X))
                gv(lambda E: E.reduce_sum(stats[:, 1:2], xs[1], axis=AX.X))
                gv(lambda E: E.tensor_tensor(
                    stats[:, 4:5], stats[:, 0:1], stats[:, 1:2], ALU.add))
                gv(lambda E: E.tensor_scalar(
                    stats[:, 4:5], stats[:, 4:5], 1.0 / E_, None, ALU.mult))
                gv(lambda E: E.tensor_tensor(tmp, xs[0], xs[0], ALU.mult))
                gv(lambda E: E.reduce_sum(stats[:, 2:3], tmp, axis=AX.X))
                gv(lambda E: E.tensor_tensor(tmp, xs[1], xs[1], ALU.mult))
                gv(lambda E: E.reduce_sum(stats[:, 3:4], tmp, axis=AX.X))
                gv(lambda E: E.tensor_tensor(
                    stats[:, 5:6], stats[:, 2:3], stats[:, 3:4], ALU.add))
                gv(lambda E: E.tensor_scalar(
                    stats[:, 5:6], stats[:, 5:6], 1.0 / E_, None, ALU.mult))
                gv(lambda E: E.tensor_tensor(
                    stats[:, 6:7], stats[:, 4:5], stats[:, 4:5], ALU.mult))
                gv(lambda E: E.tensor_tensor(
                    stats[:, 7:8], stats[:, 5:6], stats[:, 6:7], ALU.subtract))
                var_done = gv(lambda E: E.tensor_scalar(
                    stats[:, 7:8], stats[:, 7:8], EPS, None, ALU.add))

                P.wait("scalar", "sDVE", var_done)
                last_sqrt = P.inc("scalar", lambda E: E.activation(
                    stats[:, 8:9], stats[:, 7:8], AF.Sqrt, bias=0.0, scale=1.0),
                    "sACT", 1)

                P.wait("vector", "sACT", last_sqrt)
                gv(lambda E: E.reciprocal(stats[:, 9:10], stats[:, 8:9]))
                for half in range(2):
                    P.inc("vector", lambda E, h=half: E.tensor_scalar(
                        xt[:, h * 2048:(h + 1) * 2048], xs[h],
                        stats[:, 4:5], stats[:, 9:10], ALU.subtract, ALU.mult),
                        "sDVE", 1)
                last_norm = P.cnt["sDVE"]

                evac_done = P.cnt["sDVE"]
                for g in range(4):
                    P.wait("tensor", "sDVE", last_norm if g == 0 else evac_done)

                    def tgrp(E, g=g):
                        for j in range(8):
                            e_c = g * 8 + j
                            ins = E.transpose(psT[:, j * 128:(j + 1) * 128],
                                              xt[:, e_c * 128:(e_c + 1) * 128],
                                              ident[:])
                        return ins
                    gdone = P.inc("tensor", tgrp, "sPE", 1)

                    P.wait("vector", "sPE", gdone)
                    dst3 = xn_dst.rearrange("p (e t) -> p e t", t=TOK)[
                        :, g * 8:(g + 1) * 8, t_c * 128:(t_c + 1) * 128]
                    src3 = psT[:].rearrange("p (e t) -> p e t", t=128)
                    evac_done = P.inc("vector", lambda E, d=dst3, s=src3:
                                      E.tensor_copy(d, s), "sDVE", 1)
                last_cons = P.cnt["sPE"]
            return P.cnt["sDVE"]

        E_ = float(E)

        mark("p0")
        P.wait("sync", "sLoad", consts_ld)
        pa_dve = emit_ln(x_oth, bigx)

        mark("pa")
        strips = AR(STG, 16, BF16)
        kraw = AR(STG + 16, 4, BF16)
        krot = AR(STG + 20, 4, BF16)
        ropescr = AR(STG + 24, 2, BF16)

        def strip(e_c):
            return strips[:, e_c * 256:(e_c + 1) * 256]

        def emit_rope(raw, rot, scr, cos_t, sin_t):
            r3 = raw.rearrange("p (t r) -> p t r", r=256)
            w3 = rot.rearrange("p (t r) -> p t r", r=256)
            c3 = cos_t.rearrange("p (t d) -> p t d", d=64)
            s3 = sin_t.rearrange("p (t d) -> p t d", d=64)
            sc = scr.rearrange("p (t r) -> p t r", r=128)

            def gv(fn):
                t = P.inc("vector", fn, "sDVE", 1)
                P.wait("vector", "sDVE", t)
                return t
            P.wait("vector", "sDVE", P.cnt["sDVE"])
            for hh in range(2):
                lo = r3[:, :, hh * 128:hh * 128 + 64]
                hi = r3[:, :, hh * 128 + 64:hh * 128 + 128]
                olo = w3[:, :, hh * 128:hh * 128 + 64]
                ohi = w3[:, :, hh * 128 + 64:hh * 128 + 128]
                t0 = sc[:, :, hh * 64:(hh + 1) * 64]
                gv(lambda E, a=t0, b=hi, c=s3: E.tensor_tensor(a, b, c, ALU.mult))
                gv(lambda E, a=olo, b=lo, c=c3: E.tensor_tensor(a, b, c, ALU.mult))
                gv(lambda E, a=olo, b=t0: E.tensor_tensor(a, a, b, ALU.subtract))
                gv(lambda E, a=t0, b=lo, c=s3: E.tensor_tensor(a, b, c, ALU.mult))
                gv(lambda E, a=ohi, b=hi, c=c3: E.tensor_tensor(a, b, c, ALU.mult))
                ret = gv(lambda E, a=ohi, b=t0: E.tensor_tensor(a, a, b, ALU.add))
            return ret

        kv_evac = 0
        strips_free = 0
        for p in range(4):
            P.wait("sync", "sPk", prep["wk"])
            if p == 0:
                P.wait("sync", "sDVE", pa_dve)
            P.wait("sync", "sPE", strips_free)
            for e_c in range(ECH):
                P.inc("sync", lambda E, e=e_c, p=p: E.dma_start(
                    out=strip(e), in_=wk_scr[e * 128:(e + 1) * 128, p * 256:(p + 1) * 256]),
                    "sLoad", 16)
            sld = P.cnt["sLoad"]

            for wave in range(2):
                P.wait("tensor", "sLoad", sld)
                P.wait("tensor", "sDVE", max(kv_evac, pa_dve))

                def kproj(E, p=p, wave=wave):
                    for e_c in range(ECH):
                        for tt in range(4):
                            ins = E.matmul(PS4[tt],
                                           xn_tile(bigx, e_c, wave * 4 + tt), strip(e_c),
                                           start=(e_c == 0), stop=(e_c == ECH - 1))
                    return ins
                wdone = P.inc("tensor", kproj, "sPE", 1)

                P.wait("vector", "sPE", wdone)

                def kevacf(E, wave=wave):
                    for tt in range(4):
                        t_c = wave * 4 + tt
                        ins = E.tensor_scalar(kraw[:, t_c * 256:(t_c + 1) * 256],
                                              PS4[tt], ISC, None, ALU.mult)
                    return ins
                kv_evac = P.inc("vector", kevacf, "sDVE", 1)
            strips_free = P.cnt["sPE"]

            rope_done = emit_rope(kraw, krot, ropescr, cosk, sink)

            for hh in range(2):
                P.wait("tensor", "sDVE", P.cnt["sDVE"])

                def ktp(E, hh=hh):
                    for t_c in range(8):
                        ins = E.transpose(psT[:, t_c * 128:(t_c + 1) * 128],
                                          krot[:, t_c * 256 + hh * 128:
                                               t_c * 256 + (hh + 1) * 128],
                                          ident[:])
                    return ins
                tp = P.inc("tensor", ktp, "sPE", 1)
                P.wait("vector", "sPE", tp)
                head = 2 * p + hh
                P.inc("vector", lambda E, h=head: E.tensor_copy(
                    k_fm[:, h * TOK:(h + 1) * TOK], psT[:]), "sDVE", 1)

        v_evac = 0
        for p in range(4):
            P.wait("sync", "sPv", prep["wv"])
            P.wait("sync", "sPE", P.cnt["sPE"])
            for e_c in range(ECH):
                P.inc("sync", lambda E, e=e_c, p=p: E.dma_start(
                    out=strip(e), in_=wv_scr[e * 128:(e + 1) * 128, p * 256:(p + 1) * 256]),
                    "sLoad", 16)
            sld = P.cnt["sLoad"]
            for wave in range(2):
                P.wait("tensor", "sLoad", sld)
                P.wait("tensor", "sDVE", max(v_evac, pa_dve))

                def vproj(E, p=p, wave=wave):
                    for e_c in range(ECH):
                        for tt in range(4):
                            t_c = wave * 4 + tt
                            ins = E.matmul(PS4[tt],
                                           xn_tile(bigx, e_c, t_c), strip(e_c),
                                           start=(e_c == 0), stop=(e_c == ECH - 1))
                    return ins
                vp = P.inc("tensor", vproj, "sPE", 1)

                P.wait("vector", "sPE", vp)

                def vev(E, p=p, wave=wave):
                    for tt in range(4):
                        t_c = wave * 4 + tt
                        ins = E.tensor_copy(
                            v_tok[:, t_c * 1024 + p * 256: t_c * 1024 + (p + 1) * 256],
                            PS4[tt])
                    return ins
                v_evac = P.inc("vector", vev, "sDVE", 1)
        pb_pe = P.cnt["sPE"]

        mark("pb")
        pa2_dve = emit_ln(x_own, xn_own, entry_pe=pb_pe)

        for t in ("xn_own", "xn_oth", "k_fm", "v_tok"):
            if t in tap_d:
                src = {"xn_own": xn_own, "xn_oth": bigx, "k_fm": k_fm,
                       "v_tok": v_tok}[t]
                P.wait("gpsimd", "sDVE", pa2_dve)
                P.wait("gpsimd", "sPE", P.cnt["sPE"])
                P.inc("gpsimd", lambda E, s=src, d=tap_d[t], w=tap_shapes[t][1]:
                      E.dma_start(out=d, in_=s[:, 0:w]), "sGp", 16)
        early_tap_gp = P.cnt["sGp"]
        if tap_d:
            P.wait("sync", "sGp", early_tap_gp)

        mark("pa2")
        attn_fm = bigx
        wqstrips = AR(STG, 16, BF16)
        s_sl = [AR(STG + 16, 4, F32), AR(STG + 20, 4, F32)]
        p_sl = [AR(STG + 24, 2, BF16), AR(STG + 26, 2, BF16)]
        pt_sl = [AR(STG + 28, 2, BF16), AR(STG + 30, 2, BF16)]
        qraw = AR(STG + 16, 4, BF16)
        qrot = AR(STG + 20, 4, BF16)
        qscr = AR(STG + 24, 2, BF16)
        psS = [psBf, psCf]
        psS_bf = [psB[:], psC[:]]
        PSQ = [psA[:, 0:256], psA[:, 512:768]]

        wq_3d = wq_scr.rearrange("(e p) o -> p e o", p=128)
        wqst_3d = wqstrips.rearrange("p (e o) -> p e o", o=256)

        def wqs(e_c):
            return wqstrips[:, e_c * 256:(e_c + 1) * 256]

        q_evac = 0
        strips_free = 0
        mx_, exp_, pn_, ptp_, ptc_, pv_, ae_ = {}, {}, {}, {}, {}, {}, {}
        ci = 0

        def emit_chain_head(c, hh, qi, kv, qfm_gate):
            N = (8 - qi) * 128
            sl = c % 2
            P.wait("tensor", "sDVE", max(qfm_gate, ptc_.get(c - 2, 0)))
            P.wait("tensor", "sACT", exp_.get(c - 2, 0))

            def scmm(E, hh=hh, qi=qi, kv=kv, N=N, sl=sl):
                q_t = q_pair[:, hh * 1024 + qi * 128: hh * 1024 + (qi + 1) * 128]
                for off, n in ((128, min(384, N - 128)), (512, N - 512)):
                    if n > 0:
                        ins = E.matmul(
                            psS[sl][:, off:off + n], q_t,
                            k_fm[:, kv * 1024 + qi * 128 + off:
                                 kv * 1024 + qi * 128 + off + n],
                            start=True, stop=True)
                ins = E.matmul(psS[sl][:, 0:128], q_t,
                               k_fm[:, kv * 1024 + qi * 128:
                                    kv * 1024 + (qi + 1) * 128],
                               start=True, stop=False)
                ins = E.matmul(psS[sl][:, 0:128], ident[:], mask[:],
                               start=False, stop=True)
                return ins
            return P.inc("tensor", scmm, "sPE", 1)

        def chain_B(c, qi, sc_done):
            N = (8 - qi) * 128
            sl = c % 2
            pS = p_sl[sl]
            mx = stats[:, 10 + sl:11 + sl]
            nmx = stats[:, 16 + sl:17 + sl]
            sm = stats[:, 12 + sl:13 + sl]

            P.wait("vector", "sPE", sc_done)
            t = P.inc("vector", lambda E, N=N: E.reduce_max(
                mx, psS[sl][:, 0:N], axis=AX.X), "sDVE", 1)
            P.wait("vector", "sDVE", t)
            mx_[c] = P.inc("vector", lambda E: E.tensor_scalar(
                nmx, mx, -1.0, None, ALU.mult), "sDVE", 1)

            P.wait("scalar", "sDVE", mx_[c])
            P.wait("scalar", "sPE", ptp_.get(c - 2, 0))
            exp_[c] = P.inc("scalar", lambda E, N=N: E.activation(
                pS[:, 0:N], psS[sl][:, 0:N], AF.Exp, bias=nmx, scale=1.0,
                accum_out=sm), "sACT", 1)

        def chain_C(c, qi):
            N = (8 - qi) * 128
            Ta = 8 - qi
            sl = c % 2
            pS = p_sl[sl]
            sm = stats[:, 12 + sl:13 + sl]
            rc = stats[:, 14 + sl:15 + sl]
            P.wait("vector", "sACT", exp_[c])
            t = P.inc("vector", lambda E: E.reciprocal(rc, sm), "sDVE", 1)
            P.wait("vector", "sDVE", t)
            pn_[c] = P.inc("vector", lambda E, N=N: E.tensor_scalar(
                pS[:, 0:N], pS[:, 0:N], rc, None, ALU.mult), "sDVE", 1)

            P.wait("tensor", "sDVE", pn_[c])

            def ptp(E, Ta=Ta, sl=sl):
                for j in range(Ta):
                    ins = E.transpose(psS_bf[sl][:, j * 128:(j + 1) * 128],
                                      pS[:, j * 128:(j + 1) * 128], ident[:])
                return ins
            ptp_[c] = P.inc("tensor", ptp, "sPE", 1)

        def chain_Cp(c, qi):
            N = (8 - qi) * 128
            sl = c % 2
            P.wait("vector", "sPE", max(ptp_[c], pv_.get(c - 2, 0)))
            ptc_[c] = P.inc("vector", lambda E, N=N, sl=sl: E.tensor_copy(
                pt_sl[sl][:, 0:N], psS_bf[sl][:, 0:N]), "sDVE", 1)

        def chain_E(c, qi, kv, head):
            Ta = 8 - qi
            sl = c % 2
            P.wait("tensor", "sDVE", ptc_[c])
            P.wait("tensor", "sACT", ae_.get(c - 1, 0))

            def pvmm(E, qi=qi, Ta=Ta, kv=kv, sl=sl):
                for j in range(Ta):
                    kj = qi + j
                    ins = E.matmul(
                        psD[:, 0:128],
                        v_tok[:, kj * 1024 + kv * 128: kj * 1024 + (kv + 1) * 128],
                        pt_sl[sl][:, j * 128:(j + 1) * 128],
                        start=(j == 0), stop=(j == Ta - 1))
                return ins
            pv_[c] = P.inc("tensor", pvmm, "sPE", 1)

            P.wait("scalar", "sPE", pv_[c])
            ae_[c] = P.inc("scalar", lambda E, head=head, qi=qi: E.activation(
                attn_fm[:, head * 1024 + qi * 128: head * 1024 + (qi + 1) * 128],
                psD[:, 0:128], AF.Copy, bias=0.0, scale=1.0), "sACT", 1)

        for PP in range(16):
            P.wait("sync", "sPq", prep["wq"])
            if PP == 0:
                P.wait("sync", "sDVE", pa2_dve)
                P.wait("sync", "sPE", pb_pe)
            P.wait("sync", "sPE", strips_free)
            for e0 in range(0, ECH, 8):
                P.inc("sync", lambda E, e0=e0, q=PP: E.dma_start(
                    out=wqst_3d[:, e0:e0 + 8, :],
                    in_=wq_3d[:, e0:e0 + 8, q * 256:(q + 1) * 256]), "sLoad", 16)
            sld = P.cnt["sLoad"]

            for wave in range(4):
                P.wait("tensor", "sLoad", sld)
                P.wait("tensor", "sDVE", max(q_evac, pa2_dve))

                def qproj(E, q=PP, wave=wave):
                    for e_c in range(ECH):
                        for tt in range(2):
                            ins = E.matmul(PSQ[tt],
                                           xn_tile(xn_own, e_c, wave * 2 + tt), wqs(e_c),
                                           start=(e_c == 0), stop=(e_c == ECH - 1))
                    return ins
                wdone = P.inc("tensor", qproj, "sPE", 1)

                P.wait("vector", "sPE", wdone)
                P.wait("vector", "sACT", ae_.get(ci - 1, 0))

                def qevac(E, wave=wave):
                    for tt in range(2):
                        t_c = wave * 2 + tt
                        ins = E.tensor_copy(qraw[:, t_c * 256:(t_c + 1) * 256],
                                            PSQ[tt])
                    return ins
                q_evac = P.inc("vector", qevac, "sDVE", 1)
            strips_free = P.cnt["sPE"]

            rope_done = emit_rope(qraw, qrot, qscr, cosq, sinq)

            for hh in range(2):
                P.wait("tensor", "sDVE", P.cnt["sDVE"])

                def qtp(E, hh=hh):
                    for t_c in range(8):
                        ins = E.transpose(psT[:, t_c * 128:(t_c + 1) * 128],
                                          qrot[:, t_c * 256 + hh * 128:
                                               t_c * 256 + (hh + 1) * 128],
                                          ident[:])
                    return ins
                tp = P.inc("tensor", qtp, "sPE", 1)
                P.wait("vector", "sPE", tp)
                P.inc("vector", lambda E, hh=hh: E.tensor_copy(
                    q_pair[:, hh * 1024:(hh + 1) * 1024], psT[:]), "sDVE", 1)
            qfm_done = P.cnt["sDVE"]

            if "q_fm" in tap_d and PP == 0:
                P.wait("gpsimd", "sDVE", qfm_done)
                P.inc("gpsimd", lambda E: E.dma_start(out=tap_d["q_fm"], in_=q_pair[:]),
                      "sGp", 16)

            chains = [(ci + 8 * hh + qi, hh, qi, (2 * PP + hh) // GQ,
                       2 * PP + hh) for hh in range(2) for qi in range(8)]
            sc_ = {}
            for i in range(len(chains) + 3):
                if 0 <= i - 2 < len(chains):
                    c, hh, qi, kv, head = chains[i - 2]
                    chain_C(c, qi)
                if 0 <= i - 1 < len(chains):
                    c, hh, qi, kv, head = chains[i - 1]
                    chain_B(c, qi, sc_[c])
                if 0 <= i - 2 < len(chains):
                    c, hh, qi, kv, head = chains[i - 2]
                    chain_Cp(c, qi)
                if i < len(chains):
                    c, hh, qi, kv, head = chains[i]
                    sc_[c] = emit_chain_head(c, hh, qi, kv, qfm_done)
                if 0 <= i - 3 < len(chains):
                    c, hh, qi, kv, head = chains[i - 3]
                    chain_E(c, qi, kv, head)
            ci += len(chains)
        attn_done = ae_[ci - 1]

        for t in ("attn_fm",):
            if t in tap_d:
                src = attn_fm
                P.wait("gpsimd", "sACT", attn_done)
                P.wait("gpsimd", "sDVE", P.cnt["sDVE"])
                P.inc("gpsimd", lambda E, s=src, d=tap_d[t], w=tap_shapes[t][1]:
                      E.dma_start(out=d, in_=s[:, 0:w]), "sGp", 16)

        mark("pc")
        wostrips = AR(STG, 16, BF16)
        x_sb = [AR(STG + 16, 1, F32), AR(STG + 17, 1, F32)]
        o_sb = [AR(STG + 18, 1, F32), AR(STG + 19, 1, F32)]
        PSW = [psA[:, 0:256], psA[:, 512:768], psBf[:, 0:256], psBf[:, 512:768]]

        def wos(o_c):
            return wostrips[:, o_c * 256:(o_c + 1) * 256]

        strips_free = 0
        wo_mm = {}
        wo_evac = {}
        wo_dma = {}
        for ef in range(16):
            P.wait("sync", "sPo", prep["wo"])
            if ef == 0:
                P.wait("sync", "sACT", attn_done)
                P.wait("sync", "sDVE", P.cnt["sDVE"])
            P.wait("sync", "sPE", strips_free)
            for o_c in range(ECH):
                P.inc("sync", lambda E, o=o_c, ef=ef: E.dma_start(
                    out=wos(o), in_=wo_scr[o * 128:(o + 1) * 128, ef * 256:(ef + 1) * 256]),
                    "sLoad", 16)
            sld = P.cnt["sLoad"]

            for t_c in range(8):
                k = ef * 8 + t_c
                xsem = "sX0" if t_c % 2 == 0 else "sX1"
                P.wait("sync", "sDVE", wo_evac.get(k - 2, 0))
                P.inc("sync", lambda E, t=t_c, ef=ef: E.dma_start(
                    out=x_sb[t % 2],
                    in_=x_own[t * 128:(t + 1) * 128, ef * 256:(ef + 1) * 256]),
                    xsem, 16)
                xld = (xsem, P.cnt[xsem])

                P.wait("tensor", "sLoad", sld)
                P.wait("tensor", "sDVE", wo_evac.get(k - 4, 0))

                def womm(E, ef=ef, t_c=t_c, k=k):
                    ps = PSW[k % 4]
                    for o_c in range(ECH):
                        ins = E.matmul(ps, attn_fm[:, o_c * 1024 + t_c * 128:
                                                   o_c * 1024 + (t_c + 1) * 128],
                                       wos(o_c), start=(o_c == 0), stop=(o_c == ECH - 1))
                    return ins
                wo_mm[k] = P.inc("tensor", womm, "sPE", 1)

                P.wait("vector", "sPE", wo_mm[k])
                P.wait("vector", xld[0], xld[1])
                P.wait("vector", "sGp", wo_dma.get(k - 1, 0))
                wo_evac[k] = P.inc("vector", lambda E, k=k, t=t_c: E.tensor_tensor(
                    o_sb[t % 2], PSW[k % 4], x_sb[t % 2], ALU.add), "sDVE", 1)

                P.wait("gpsimd", "sDVE", wo_evac[k])
                wo_dma[k] = P.inc("gpsimd", lambda E, t=t_c, ef=ef: E.dma_start(
                    out=out_d[t * 128:(t + 1) * 128, ef * 256:(ef + 1) * 256],
                    in_=o_sb[t % 2]), "sGp", 16)
            strips_free = P.cnt["sPE"]
        out_init_gp = P.cnt["sGp"]

        mark("pw")
        g16st = AR(64, 16, I16)
        u16st = AR(80, 16, I16)
        gbf = [AR(96, 16, BF16), AR(112, 16, BF16)]
        ubf = [AR(128, 16, BF16), AR(144, 16, BF16)]
        gate_sb = AR(STG, 2, BF16)
        up_sb = AR(STG + 2, 2, BF16)
        int_sb = [AR(STG + 4, 2, BF16), AR(STG + 6, 2, BF16)]

        gconv = {}
        uconv = {}
        gmm = {}
        umm = {}
        silu_done = {}
        upev = {}
        ttd = {}
        int_dma = {}
        pw_dve = P.cnt["sDVE"]
        g16_3d = g16.rearrange("(e p) i -> p e i", p=128)
        u16_3d = u16.rearrange("(e p) i -> p e i", p=128)
        g16st_3d = g16st[:].rearrange("p (e i) -> p e i", i=256)
        u16st_3d = u16st[:].rearrange("p (e i) -> p e i", i=256)
        for sb in range(NSB):
            slot = sb % 2
            P.wait("sync", "sDVE", gconv.get(sb - 1, 0))
            P.wait("gpsimd", "sACT", uconv.get(sb - 1, 0))
            if sb == 0:
                P.wait("sync", "sGp", out_init_gp)
                P.wait("sync", "sPE", P.cnt["sPE"])
                P.wait("gpsimd", "sPE", P.cnt["sPE"])
            for e0 in range(0, ECH, 8):
                P.inc("sync", lambda E, e0=e0, sb=sb: E.dma_start(
                    out=g16st_3d[:, e0:e0 + 8, :],
                    in_=g16_3d[:, e0:e0 + 8, sb * 256:(sb + 1) * 256]),
                    "sG", 16)
            gld = P.cnt["sG"]
            for e0 in range(0, ECH, 8):
                P.inc("gpsimd", lambda E, e0=e0, sb=sb: E.dma_start(
                    out=u16st_3d[:, e0:e0 + 8, :],
                    in_=u16_3d[:, e0:e0 + 8, sb * 256:(sb + 1) * 256]),
                    "sU", 16)
            uld = P.cnt["sU"]

            P.wait("vector", "sG", gld)
            P.wait("vector", "sPE", gmm.get(sb - 2, 0))
            gconv[sb] = P.inc("vector", lambda E, slot=slot: E.tensor_copy(
                gbf[slot][:], g16st[:]), "sDVE", 1)
            P.wait("scalar", "sU", uld)
            P.wait("scalar", "sPE", umm.get(sb - 2, 0))
            uconv[sb] = P.inc("scalar", lambda E, slot=slot: E.activation(
                ubf[slot][:], u16st[:], AF.Copy, bias=0.0, scale=1.0), "sACT", 1)

            for ic in range(2):
                icg = sb * 2 + ic
                psg = psA[:] if ic == 0 else psBf
                P.wait("tensor", "sDVE", max(gconv[sb], pw_dve))
                P.wait("tensor", "sACT", silu_done.get(icg - 2, 0))

                def gmmf(E, slot=slot, ic=ic, psg=psg):
                    for e_c in range(ECH):
                        for tg in range(2):
                            ins = E.matmul(psg[:, tg * 512:(tg + 1) * 512],
                                           gbf[slot][:, e_c * 256 + ic * 128:
                                                      e_c * 256 + (ic + 1) * 128],
                                           xn_half(xn_own, e_c, tg),
                                           start=(e_c == 0), stop=(e_c == ECH - 1))
                    return ins
                gmm[sb] = P.inc("tensor", gmmf, "sPE", 1)
                gmm_t = gmm[sb]
                P.wait("tensor", "sACT", uconv[sb])
                P.wait("tensor", "sDVE", upev.get(icg - 1, 0))

                def ummf(E, slot=slot, ic=ic):
                    for e_c in range(ECH):
                        for tg in range(2):
                            ins = E.matmul(psCf[:, tg * 512:(tg + 1) * 512],
                                           ubf[slot][:, e_c * 256 + ic * 128:
                                                      e_c * 256 + (ic + 1) * 128],
                                           xn_half(xn_own, e_c, tg),
                                           start=(e_c == 0), stop=(e_c == ECH - 1))
                    return ins
                umm[sb] = P.inc("tensor", ummf, "sPE", 1)
                umm_t = umm[sb]

                P.wait("scalar", "sPE", gmm_t)
                P.wait("scalar", "sDVE", ttd.get(icg - 1, 0))
                silu_done[icg] = P.inc("scalar", lambda E, icg=icg, psg=psg:
                                       E.activation(gate_sb[:], psg[:], AF.Silu,
                                                    bias=0.0,
                                                    scale=gs[:, icg:icg + 1]),
                                       "sACT", 1)
                P.wait("vector", "sPE", umm_t)
                P.wait("vector", "sDVE", ttd.get(icg - 1, 0))
                upev[icg] = P.inc("vector", lambda E, icg=icg: E.tensor_scalar(
                    up_sb[:], psCf, us[:, icg:icg + 1], None, ALU.mult), "sDVE", 1)
                P.wait("vector", "sACT", silu_done[icg])
                P.wait("vector", "sDVE", upev[icg])
                P.wait("vector", "sLoad", int_dma.get(icg - 1, 0))
                ttd[icg] = P.inc("vector", lambda E, icg=icg: E.tensor_tensor(
                    int_sb[icg % 2][:], gate_sb[:], up_sb[:], ALU.mult), "sDVE", 1)
                P.wait("sync", "sDVE", ttd[icg])
                int_dma[icg] = P.inc("sync", lambda E, icg=icg: E.dma_start(
                    out=inter_scr[icg * 128:(icg + 1) * 128, :],
                    in_=int_sb[icg % 2][:]), "sLoad", 16)
        pg_int_ld = P.cnt["sLoad"]
        pg_pe = P.cnt["sPE"]

        mark("pg")
        dstrips = [AR(0 + j * 8, 8, BF16) for j in range(IB_D)]
        d16stg = [AR(88, 8, I16), AR(96, 8, I16)]
        int_t = [AR(104 + 2 * j, 2, BF16) for j in range(IB_D)]
        dsb = AR(126, 16, F32)
        rows = [AR(142 + 16 * r, 16, F32) for r in range(3)]
        PS0 = [psA[:, 0:512], psA[:, 512:1024], psBf[:, 0:512], psBf[:, 512:1024]]
        PS1 = [psCf[:, 0:512], psCf[:, 512:1024], psD[:, 0:512], psTf]

        if paranoid >= 1:
            barrier()
        P.wait("sync", "sPE", pg_pe)
        P.wait("sync", "sACT", P.cnt["sACT"])
        P.wait("sync", "sDVE", P.cnt["sDVE"])
        P.wait("sync", "sLoad", pg_int_ld)
        P.inc("sync", lambda E: E.dma_start(out=dsb, in_=dsb_d), "sLoad", 16)
        dsb_ld = P.cnt["sLoad"]

        dconv = {}
        dload = {}
        iload = {}
        sup_mm = {}
        sup_ev = {}
        row_last = [0, 0, 0]
        blk_end_pe = {0: pg_pe}
        acc_total = {-1: out_init_gp}
        sup_idx = 0
        row_idx = 0
        i0 = 0
        for b, nb in enumerate(DOWN_BLOCKS):
            if paranoid >= 1:
                barrier()
            for j in range(nb):
                icg = i0 + j
                stg = j % 2
                P.wait("sync", "sDVE", dconv.get((b, j - 2), 0))
                if b > 0:
                    P.wait("sync", "sPE", blk_end_pe[b])
                dsem = "sD0" if stg == 0 else "sD1"
                P.inc("sync", lambda E, icg=icg, stg=stg: E.dma_start(
                    out=d16stg[stg][:], in_=d16[icg * 128:(icg + 1) * 128, :],
                    max_dma_last_dim=2048), dsem, 16)
                dload[(b, j)] = (dsem, P.cnt[dsem])
                P.inc("sync", lambda E, icg=icg, j=j: E.dma_start(
                    out=int_t[j][:], in_=inter_scr[icg * 128:(icg + 1) * 128, :]),
                    "sIl", 16)
                iload[(b, j)] = P.cnt["sIl"]

                P.wait("vector", dload[(b, j)][0], dload[(b, j)][1])
                if b > 0:
                    P.wait("vector", "sPE", blk_end_pe[b])
                dconv[(b, j)] = P.inc("vector", lambda E, j=j, stg=stg:
                                      E.tensor_copy(dstrips[j][:], d16stg[stg][:]),
                                      "sDVE", 1)

            if dbg_pd0 and b == 0:
                barrier()
                for j in range(nb):
                    P.inc("gpsimd", lambda G, j=j: G.dma_start(
                        out=pd0_d[j * 128:(j + 1) * 128, 0:E], in_=dstrips[j][:],
                        max_dma_last_dim=2048), "sGp", 16)
                    P.inc("gpsimd", lambda G, j=j: G.dma_start(
                        out=pd0_d[j * 128:(j + 1) * 128, E:E + TOK], in_=int_t[j][:]),
                        "sGp", 16)
                barrier()
            for tg in range(4):
                for hg in range(4):
                    if paranoid >= 2:
                        barrier()
                    ps_set = PS0 if sup_idx % 2 == 0 else PS1
                    P.wait("tensor", "sDVE", sup_ev.get(sup_idx - 2, 0))
                    if b == 0 and sup_idx == 0:
                        P.wait("tensor", "sPE", pg_pe)

                    def supmm(E, b=b, nb=nb, i0=i0, tg=tg, hg=hg, ps_set=ps_set):
                        for j in range(nb):
                            for u in range(4):
                                t_c = tg * 2 + u // 2
                                hf = hg * 2 + u % 2
                                ins = E.matmul(
                                    ps_set[u],
                                    int_t[j][:, t_c * 128:(t_c + 1) * 128],
                                    dstrips[j][:, hf * 512:(hf + 1) * 512],
                                    start=(j == 0), stop=(j == nb - 1))
                        return ins
                    P.wait("tensor", "sDVE", dconv[(b, nb - 1)])
                    P.wait("tensor", "sIl", iload[(b, nb - 1)])
                    sup_mm[sup_idx] = P.inc("tensor", supmm, "sPE", 1)

                    P.wait("vector", "sPE", sup_mm[sup_idx])
                    P.wait("vector", "sLoad", dsb_ld)
                    ra = row_idx % 3
                    rb = (row_idx + 1) % 3
                    if hg == 0:
                        P.wait("vector", "sR%d" % ra, row_last[ra])
                        P.wait("vector", "sR%d" % rb, row_last[rb])

                    def supev(E, tg=tg, hg=hg, ps_set=ps_set, ra=ra, rb=rb):
                        for u in range(4):
                            t_loc = u // 2
                            hf = hg * 2 + u % 2
                            r = rows[ra] if t_loc == 0 else rows[rb]
                            ins = E.tensor_tensor(
                                r[:, hf * 512:(hf + 1) * 512], ps_set[u],
                                dsb[:, hf * 512:(hf + 1) * 512], ALU.mult)
                        return ins
                    sup_ev[sup_idx] = P.inc("vector", supev, "sDVE", 1)
                    sup_idx += 1

                if paranoid >= 2:
                    barrier()
                P.wait("gpsimd", "sDVE", sup_ev[sup_idx - 1])
                if b == 0 and tg == 0:
                    P.wait("gpsimd", "sGp", acc_total[-1])
                for u in range(2):
                    t_c = tg * 2 + u
                    ri = (row_idx + u) % 3
                    P.inc("gpsimd", lambda E, t=t_c, r=rows[ri]: E.dma_start(
                        out=out_d[t * 128:(t + 1) * 128, :], in_=r[:],
                        accum_op=ALU.add, max_dma_last_dim=2048), "sR%d" % ri, 16)
                    if dbg_parts:
                        P.inc("gpsimd", lambda E, b=b, t=t_c, r=rows[ri]: E.dma_start(
                            out=parts_d[b * TOK + t * 128: b * TOK + (t + 1) * 128, :],
                            in_=r[:]), "sGp", 16)
                    row_last[ri] = P.cnt["sR%d" % ri]
                row_idx += 2
            blk_end_pe[b + 1] = P.cnt["sPE"]
            acc_total[b] = P.cnt["sGp"]
            i0 += nb

        P.wait("gpsimd", "sGp", P.cnt["sGp"])
        for ri in range(3):
            P.wait("gpsimd", "sR%d" % ri, P.cnt["sR%d" % ri])
        P.wait("gpsimd", "sLoad", P.cnt["sLoad"])

        mark("pd")
        if stop_phase is not None:
            m, cnts = phase_marks[stop_phase]
            for e in ENGS:
                P.ops[e] = P.ops[e][:m[e]]
            P.wait("sync", "sLoad", cnts["sLoad"])
            for s in ("sPrep", "sGp", "sPk", "sPv", "sPq", "sPo", "sG", "sU",
                      "sD0", "sD1", "sX0", "sX1", "sIl", "sR0", "sR1", "sR2"):
                P.wait("gpsimd", s, cnts[s])

        with nc.Block() as block:
            P.replay(block)

    return nc


def _rope_tok(positions):
    inv = (10000.0 ** (-(np.arange(0, D, 2, dtype=np.float32) / np.float32(D)))
           ).astype(np.float32)
    ang = positions.astype(np.float32)[:, None] * inv[None, :]
    cos = np.cos(ang).astype(bf).reshape(8, 128, 64).transpose(1, 0, 2).reshape(128, 512)
    sin = np.sin(ang).astype(bf).reshape(8, 128, 64).transpose(1, 0, 2).reshape(128, 512)
    return np.ascontiguousarray(cos), np.ascontiguousarray(sin)


def _i16(w):
    w = np.asarray(w)
    assert w.dtype == np.int32
    return np.ascontiguousarray(w.view(np.int16)[:, 0::2])


def make_in_maps(L0_attn_hidden_states, L0_attn_Wq, L0_attn_Wk, L0_attn_Wv,
                 L0_attn_Wo, L0_ffn_gate_wq, L0_ffn_up_wq, L0_ffn_down_wq,
                 L0_ffn_gate_scale, L0_ffn_up_scale, L0_ffn_down_scale):
    x = np.asarray(L0_attn_hidden_states, dtype=np.float32)
    shared = {
        "wq": np.ascontiguousarray(np.asarray(L0_attn_Wq, dtype=np.float32).T),
        "wk": np.ascontiguousarray(np.asarray(L0_attn_Wk, dtype=np.float32).T),
        "wv": np.ascontiguousarray(np.asarray(L0_attn_Wv, dtype=np.float32).T),
        "wo": np.ascontiguousarray(np.asarray(L0_attn_Wo, dtype=np.float32).T),
        "g16": np.ascontiguousarray(_i16(L0_ffn_gate_wq).T),
        "u16": np.ascontiguousarray(_i16(L0_ffn_up_wq).T),
        "d16": np.ascontiguousarray(_i16(L0_ffn_down_wq).T),
        "gs": np.ascontiguousarray(
            np.asarray(L0_ffn_gate_scale, dtype=np.float32).reshape(ICH, 128).T),
        "us": np.ascontiguousarray(
            np.asarray(L0_ffn_up_scale, dtype=np.float32).reshape(ICH, 128).T),
        "dsb": np.ascontiguousarray(np.broadcast_to(
            np.asarray(L0_ffn_down_scale, dtype=np.float32)[None, :], (128, E))),
        "mask": np.where(np.arange(128)[None, :] >= np.arange(128)[:, None],
                         np.float32(0.0), np.float32(NEG)).astype(bf),
        "ident": np.eye(128, dtype=np.float32).astype(bf),
    }
    in_maps = []
    for c in range(8):
        b, h = c // 2, c % 2
        xo = x[b, h * TOK:(h + 1) * TOK]
        xt = x[b, (1 - h) * TOK:(2 - h) * TOK]
        if h == 1:
            xo = xo[::-1]
            xt = xt[::-1]
            pos_q = 2047 - np.arange(TOK)
            pos_k = 1023 - np.arange(TOK)
        else:
            pos_q = np.arange(TOK)
            pos_k = 1024 + np.arange(TOK)
        cq, sq = _rope_tok(pos_q)
        ck, sk = _rope_tok(pos_k)
        m = dict(shared)
        m["x_own"] = np.ascontiguousarray(xo)
        m["x_oth"] = np.ascontiguousarray(xt)
        m["cosq"], m["sinq"], m["cosk"], m["sink"] = cq, sq, ck, sk
        in_maps.append(m)
    return in_maps


def kernel(**inputs):
    in_maps = make_in_maps(**inputs)
    nc = build_nc()
    res = run_bass_kernel_spmd(nc, in_maps, core_ids=list(range(8)))
    out = np.empty((B, S, E), dtype=np.float32)
    for c in range(8):
        b, h = c // 2, c % 2
        o = np.asarray(res.results[c]["out"])
        if h == 1:
            o = o[::-1]
        out[b, h * TOK:(h + 1) * TOK] = o
    return out


# version 12 (final)
# speedup vs baseline: 1.1478x; 1.1478x over previous
import contextlib
import numpy as np
import ml_dtypes
import concourse.bass as bass
import concourse.mybir as mybir
from concourse.bass_utils import run_bass_kernel_spmd

F32 = mybir.dt.float32
BF16 = mybir.dt.bfloat16
I16 = mybir.dt.int16
U8 = mybir.dt.uint8
AF = mybir.ActivationFunctionType
ALU = mybir.AluOpType
AX = mybir.AxisListType
bf = ml_dtypes.bfloat16

B, S, E = 4, 2048, 4096
TOK = 1024
ECH = E // 128
HQ, HKV, D = 32, 8, 128
GQ = HQ // HKV
II = 11008
ICH = II // 128
NSB = II // 256
EPS = 1e-5
ISC = float(1.0 / np.sqrt(128.0))
NEG = -30000.0
IB_D = 11
DOWN_BLOCKS = []
_s = 0
while _s < ICH:
    DOWN_BLOCKS.append(min(IB_D, ICH - _s))
    _s += IB_D
KB = 1024

ENGS = ("sync", "tensor", "vector", "scalar", "gpsimd")


class Plan:

    def __init__(self, sems):
        self.ops = {e: [] for e in ENGS}
        self.sems = sems
        self.cnt = {k: 0 for k in sems}

    def wait(self, eng, sem, val):
        if val <= 0:
            return
        self.ops[eng].append(lambda E, s=self.sems[sem], v=val: E.wait_ge(s, v))

    def emit(self, eng, fn):
        self.ops[eng].append(fn)

    def inc(self, eng, fn, sem, n):
        self.cnt[sem] += n
        self.ops[eng].append(
            lambda E, f=fn, s=self.sems[sem], m=n: f(E).then_inc(s, m))
        return self.cnt[sem]

    def replay(self, block):
        for eng in ENGS:
            lst = self.ops[eng]
            if not lst:
                continue

            def section(E, lst=lst):
                for fn in lst:
                    fn(E)
            getattr(block, eng)(section)


def build_nc(taps=(), stop_phase=None, dbg_inter=False, paranoid=0, dbg_parts=False,
             dbg_pd0=False):
    nc = bass.Bass("TRN2", target_bir_lowering=False)

    x_own = nc.dram_tensor("x_own", [TOK, E], F32, kind="ExternalInput").ap()
    x_oth = nc.dram_tensor("x_oth", [TOK, E], F32, kind="ExternalInput").ap()
    wq = nc.dram_tensor("wq", [E, E], F32, kind="ExternalInput").ap()
    wk = nc.dram_tensor("wk", [E, HKV * D], F32, kind="ExternalInput").ap()
    wv = nc.dram_tensor("wv", [E, HKV * D], F32, kind="ExternalInput").ap()
    wo = nc.dram_tensor("wo", [HQ * D, E], F32, kind="ExternalInput").ap()
    g16 = nc.dram_tensor("g16", [E, II], I16, kind="ExternalInput").ap()
    u16 = nc.dram_tensor("u16", [E, II], I16, kind="ExternalInput").ap()
    d16 = nc.dram_tensor("d16", [II, E], I16, kind="ExternalInput").ap()
    gs_d = nc.dram_tensor("gs", [128, ICH], F32, kind="ExternalInput").ap()
    us_d = nc.dram_tensor("us", [128, ICH], F32, kind="ExternalInput").ap()
    dsb_d = nc.dram_tensor("dsb", [128, E], F32, kind="ExternalInput").ap()
    cosq_d = nc.dram_tensor("cosq", [128, 512], BF16, kind="ExternalInput").ap()
    sinq_d = nc.dram_tensor("sinq", [128, 512], BF16, kind="ExternalInput").ap()
    cosk_d = nc.dram_tensor("cosk", [128, 512], BF16, kind="ExternalInput").ap()
    sink_d = nc.dram_tensor("sink", [128, 512], BF16, kind="ExternalInput").ap()
    mask_d = nc.dram_tensor("mask", [128, 128], BF16, kind="ExternalInput").ap()
    ident_d = nc.dram_tensor("ident", [128, 128], BF16, kind="ExternalInput").ap()
    out_d = nc.dram_tensor("out", [TOK, E], F32, kind="ExternalOutput").ap()

    wq_scr = nc.dram_tensor("wq_scr", [E, E], BF16, kind="Internal").ap()
    wk_scr = nc.dram_tensor("wk_scr", [E, HKV * D], BF16, kind="Internal").ap()
    wv_scr = nc.dram_tensor("wv_scr", [E, HKV * D], BF16, kind="Internal").ap()
    wo_scr = nc.dram_tensor("wo_scr", [HQ * D, E], BF16, kind="Internal").ap()
    inter_scr = nc.dram_tensor("inter_scr", [II, TOK], BF16,
                               kind="ExternalOutput" if dbg_inter else "Internal").ap()
    parts_d = (nc.dram_tensor("parts", [8 * TOK, E], F32, kind="ExternalOutput").ap()
               if dbg_parts else None)
    pd0_d = (nc.dram_tensor("pd0", [11 * 128, E + TOK], BF16, kind="ExternalOutput").ap()
             if dbg_pd0 else None)

    tap_shapes = {"xn_own": [128, ECH * TOK], "xn_oth": [128, ECH * TOK],
                  "k_fm": [128, HKV * TOK], "v_tok": [128, 8 * HKV * 128],
                  "attn_fm": [128, HQ * TOK], "q_fm": [128, 2 * TOK]}
    tap_d = {t: nc.dram_tensor("tap_" + t, tap_shapes[t], BF16,
                               kind="ExternalOutput").ap() for t in taps}

    st = contextlib.ExitStack()
    with st:
        cosq = st.enter_context(nc.sbuf_tensor("s_cosq", [128, 512], BF16))
        sinq = st.enter_context(nc.sbuf_tensor("s_sinq", [128, 512], BF16))
        cosk = st.enter_context(nc.sbuf_tensor("s_cosk", [128, 512], BF16))
        sink = st.enter_context(nc.sbuf_tensor("s_sink", [128, 512], BF16))
        mask = st.enter_context(nc.sbuf_tensor("s_mask", [128, 128], BF16))
        ident = st.enter_context(nc.sbuf_tensor("s_ident", [128, 128], BF16))
        gs = st.enter_context(nc.sbuf_tensor("s_gs", [128, ICH], F32))
        us = st.enter_context(nc.sbuf_tensor("s_us", [128, ICH], F32))
        stats = st.enter_context(nc.sbuf_tensor("stats", [128, 32], F32))
        q_pair = st.enter_context(nc.sbuf_tensor("q_pair", [128, 2048], BF16))
        arena = st.enter_context(nc.sbuf_tensor("arena", [128, 192 * KB], U8))
        psA = st.enter_context(nc.psum_tensor("psA", [128, 1024], F32))
        psB = st.enter_context(nc.psum_tensor("psB", [128, 2048], BF16))
        psC = st.enter_context(nc.psum_tensor("psC", [128, 2048], BF16))
        psD = st.enter_context(nc.psum_tensor("psD", [128, 512], F32))
        psT = st.enter_context(nc.psum_tensor("psT", [128, 1024], BF16))
        sem_h = {}
        for s in ("sLoad", "sPrep", "sGp", "sPE", "sDVE", "sACT",
                  "sPk", "sPv", "sPq", "sPo", "sG", "sU", "sD0", "sD1",
                  "sX0", "sX1", "sIl", "sR0", "sR1", "sR2"):
            sem_h[s] = st.enter_context(nc.semaphore(s))

        psTf = psT[:].bitcast(F32)
        psBf = psB[:].bitcast(F32)
        psCf = psC[:].bitcast(F32)
        PS4 = [psA[:, 0:256], psA[:, 512:768], psBf[:, 0:256], psBf[:, 512:768]]

        def AR(off_kb, n_kb, dt):
            v = arena[:, off_kb * KB:(off_kb + n_kb) * KB]
            return v.bitcast(dt) if dt != U8 else v

        xn_own = AR(0, 64, BF16)
        bigx = AR(64, 64, BF16)
        k_fm = AR(128, 16, BF16)
        v_tok = AR(144, 16, BF16)
        STG = 160

        P = Plan(sem_h)
        phase_marks = {}

        def barrier():
            for e in ENGS:
                for s in sem_h:
                    P.wait(e, s, P.cnt[s])

        def mark(name):
            phase_marks[name] = ({e: len(P.ops[e]) for e in ENGS}, dict(P.cnt))

        def xn_tile(xn, e_c, t_c):
            return xn[:, e_c * TOK + t_c * 128: e_c * TOK + (t_c + 1) * 128]

        def xn_half(xn, e_c, tg):
            return xn[:, e_c * TOK + tg * 512: e_c * TOK + (tg + 1) * 512]

        for dst, src in [(cosq, cosq_d), (sinq, sinq_d), (cosk, cosk_d),
                         (sink, sink_d), (mask, mask_d), (ident, ident_d),
                         (gs, gs_d), (us, us_d)]:
            P.inc("sync", lambda E, d=dst, s=src: E.dma_start(out=d[:], in_=s),
                  "sLoad", 16)
        consts_ld = P.cnt["sLoad"]

        prep = {}
        prep_sem = {"wk": "sPk", "wv": "sPv", "wq": "sPq", "wo": "sPo"}
        for name, dst, src, rows in [("wk", wk_scr, wk, 4096), ("wv", wv_scr, wv, 4096),
                                     ("wq", wq_scr, wq, 4096), ("wo", wo_scr, wo, 4096)]:
            for r0 in range(0, rows, 256):
                P.inc("gpsimd", lambda E, d=dst, s=src, r=r0: E.dma_start(
                    out=d[r:r + 256, :], in_=s[r:r + 256, :]), prep_sem[name], 16)
            prep[name] = P.cnt[prep_sem[name]]

        def emit_ln(xdram, xn_dst, entry_pe=0):
            xs = [AR(STG, 8, F32), AR(STG + 8, 8, F32)]
            tmp = AR(STG + 16, 8, F32)
            xt = AR(STG + 24, 8, BF16)
            last_cons = 0
            last_norm = 0
            last_sqrt = 0
            P.wait("sync", "sPE", entry_pe)
            for t_c in range(8):
                P.wait("sync", "sDVE", last_norm)
                for half in range(2):
                    P.inc("sync", lambda E, h=half, t=t_c: E.dma_start(
                        out=xs[h],
                        in_=xdram[t * 128:(t + 1) * 128, h * 2048:(h + 1) * 2048]),
                        "sLoad", 16)
                ld = P.cnt["sLoad"]

                def gv(fn):
                    t = P.inc("vector", fn, "sDVE", 1)
                    P.wait("vector", "sDVE", t)
                    return t

                P.wait("vector", "sLoad", ld)
                P.wait("vector", "sACT", last_sqrt)
                P.wait("vector", "sPE", last_cons)
                gv(lambda E: E.reduce_sum(stats[:, 0:1], xs[0], axis=AX.X))
                gv(lambda E: E.reduce_sum(stats[:, 1:2], xs[1], axis=AX.X))
                gv(lambda E: E.tensor_tensor(
                    stats[:, 4:5], stats[:, 0:1], stats[:, 1:2], ALU.add))
                gv(lambda E: E.tensor_scalar(
                    stats[:, 4:5], stats[:, 4:5], 1.0 / E_, None, ALU.mult))
                gv(lambda E: E.tensor_tensor(tmp, xs[0], xs[0], ALU.mult))
                gv(lambda E: E.reduce_sum(stats[:, 2:3], tmp, axis=AX.X))
                gv(lambda E: E.tensor_tensor(tmp, xs[1], xs[1], ALU.mult))
                gv(lambda E: E.reduce_sum(stats[:, 3:4], tmp, axis=AX.X))
                gv(lambda E: E.tensor_tensor(
                    stats[:, 5:6], stats[:, 2:3], stats[:, 3:4], ALU.add))
                gv(lambda E: E.tensor_scalar(
                    stats[:, 5:6], stats[:, 5:6], 1.0 / E_, None, ALU.mult))
                gv(lambda E: E.tensor_tensor(
                    stats[:, 6:7], stats[:, 4:5], stats[:, 4:5], ALU.mult))
                gv(lambda E: E.tensor_tensor(
                    stats[:, 7:8], stats[:, 5:6], stats[:, 6:7], ALU.subtract))
                var_done = gv(lambda E: E.tensor_scalar(
                    stats[:, 7:8], stats[:, 7:8], EPS, None, ALU.add))

                P.wait("scalar", "sDVE", var_done)
                last_sqrt = P.inc("scalar", lambda E: E.activation(
                    stats[:, 8:9], stats[:, 7:8], AF.Sqrt, bias=0.0, scale=1.0),
                    "sACT", 1)

                P.wait("vector", "sACT", last_sqrt)
                gv(lambda E: E.reciprocal(stats[:, 9:10], stats[:, 8:9]))
                for half in range(2):
                    P.inc("vector", lambda E, h=half: E.tensor_scalar(
                        xt[:, h * 2048:(h + 1) * 2048], xs[h],
                        stats[:, 4:5], stats[:, 9:10], ALU.subtract, ALU.mult),
                        "sDVE", 1)
                last_norm = P.cnt["sDVE"]

                evac_done = P.cnt["sDVE"]
                for g in range(4):
                    P.wait("tensor", "sDVE", last_norm if g == 0 else evac_done)

                    def tgrp(E, g=g):
                        for j in range(8):
                            e_c = g * 8 + j
                            ins = E.transpose(psT[:, j * 128:(j + 1) * 128],
                                              xt[:, e_c * 128:(e_c + 1) * 128],
                                              ident[:])
                        return ins
                    gdone = P.inc("tensor", tgrp, "sPE", 1)

                    P.wait("vector", "sPE", gdone)
                    dst3 = xn_dst.rearrange("p (e t) -> p e t", t=TOK)[
                        :, g * 8:(g + 1) * 8, t_c * 128:(t_c + 1) * 128]
                    src3 = psT[:].rearrange("p (e t) -> p e t", t=128)
                    evac_done = P.inc("vector", lambda E, d=dst3, s=src3:
                                      E.tensor_copy(d, s), "sDVE", 1)
                last_cons = P.cnt["sPE"]
            return P.cnt["sDVE"]

        E_ = float(E)

        mark("p0")
        P.wait("sync", "sLoad", consts_ld)
        pa_dve = emit_ln(x_oth, bigx)

        mark("pa")
        strips = AR(STG, 16, BF16)
        kraw = AR(STG + 16, 4, BF16)
        krot = AR(STG + 20, 4, BF16)
        ropescr = AR(STG + 24, 2, BF16)

        def strip(e_c):
            return strips[:, e_c * 256:(e_c + 1) * 256]

        def emit_rope(raw, rot, scr, cos_t, sin_t):
            r3 = raw.rearrange("p (t r) -> p t r", r=256)
            w3 = rot.rearrange("p (t r) -> p t r", r=256)
            c3 = cos_t.rearrange("p (t d) -> p t d", d=64)
            s3 = sin_t.rearrange("p (t d) -> p t d", d=64)
            sc = scr.rearrange("p (t r) -> p t r", r=128)

            def gv(fn):
                t = P.inc("vector", fn, "sDVE", 1)
                P.wait("vector", "sDVE", t)
                return t
            P.wait("vector", "sDVE", P.cnt["sDVE"])
            for hh in range(2):
                lo = r3[:, :, hh * 128:hh * 128 + 64]
                hi = r3[:, :, hh * 128 + 64:hh * 128 + 128]
                olo = w3[:, :, hh * 128:hh * 128 + 64]
                ohi = w3[:, :, hh * 128 + 64:hh * 128 + 128]
                t0 = sc[:, :, hh * 64:(hh + 1) * 64]
                gv(lambda E, a=t0, b=hi, c=s3: E.tensor_tensor(a, b, c, ALU.mult))
                gv(lambda E, a=olo, b=lo, c=c3: E.tensor_tensor(a, b, c, ALU.mult))
                gv(lambda E, a=olo, b=t0: E.tensor_tensor(a, a, b, ALU.subtract))
                gv(lambda E, a=t0, b=lo, c=s3: E.tensor_tensor(a, b, c, ALU.mult))
                gv(lambda E, a=ohi, b=hi, c=c3: E.tensor_tensor(a, b, c, ALU.mult))
                ret = gv(lambda E, a=ohi, b=t0: E.tensor_tensor(a, a, b, ALU.add))
            return ret

        kv_evac = 0
        strips_free = 0
        for p in range(4):
            P.wait("sync", "sPk", prep["wk"])
            if p == 0:
                P.wait("sync", "sDVE", pa_dve)
            P.wait("sync", "sPE", strips_free)
            for e_c in range(ECH):
                P.inc("sync", lambda E, e=e_c, p=p: E.dma_start(
                    out=strip(e), in_=wk_scr[e * 128:(e + 1) * 128, p * 256:(p + 1) * 256]),
                    "sLoad", 16)
            sld = P.cnt["sLoad"]

            for wave in range(2):
                P.wait("tensor", "sLoad", sld)
                P.wait("tensor", "sDVE", max(kv_evac, pa_dve))

                def kproj(E, p=p, wave=wave):
                    for e_c in range(ECH):
                        for tt in range(4):
                            ins = E.matmul(PS4[tt],
                                           xn_tile(bigx, e_c, wave * 4 + tt), strip(e_c),
                                           start=(e_c == 0), stop=(e_c == ECH - 1))
                    return ins
                wdone = P.inc("tensor", kproj, "sPE", 1)

                P.wait("vector", "sPE", wdone)

                def kevacf(E, wave=wave):
                    for tt in range(4):
                        t_c = wave * 4 + tt
                        ins = E.tensor_scalar(kraw[:, t_c * 256:(t_c + 1) * 256],
                                              PS4[tt], ISC, None, ALU.mult)
                    return ins
                kv_evac = P.inc("vector", kevacf, "sDVE", 1)
            strips_free = P.cnt["sPE"]

            rope_done = emit_rope(kraw, krot, ropescr, cosk, sink)

            for hh in range(2):
                P.wait("tensor", "sDVE", P.cnt["sDVE"])

                def ktp(E, hh=hh):
                    for t_c in range(8):
                        ins = E.transpose(psT[:, t_c * 128:(t_c + 1) * 128],
                                          krot[:, t_c * 256 + hh * 128:
                                               t_c * 256 + (hh + 1) * 128],
                                          ident[:])
                    return ins
                tp = P.inc("tensor", ktp, "sPE", 1)
                P.wait("vector", "sPE", tp)
                head = 2 * p + hh
                P.inc("vector", lambda E, h=head: E.tensor_copy(
                    k_fm[:, h * TOK:(h + 1) * TOK], psT[:]), "sDVE", 1)

        v_evac = 0
        for p in range(4):
            P.wait("sync", "sPv", prep["wv"])
            P.wait("sync", "sPE", P.cnt["sPE"])
            for e_c in range(ECH):
                P.inc("sync", lambda E, e=e_c, p=p: E.dma_start(
                    out=strip(e), in_=wv_scr[e * 128:(e + 1) * 128, p * 256:(p + 1) * 256]),
                    "sLoad", 16)
            sld = P.cnt["sLoad"]
            for wave in range(2):
                P.wait("tensor", "sLoad", sld)
                P.wait("tensor", "sDVE", max(v_evac, pa_dve))

                def vproj(E, p=p, wave=wave):
                    for e_c in range(ECH):
                        for tt in range(4):
                            t_c = wave * 4 + tt
                            ins = E.matmul(PS4[tt],
                                           xn_tile(bigx, e_c, t_c), strip(e_c),
                                           start=(e_c == 0), stop=(e_c == ECH - 1))
                    return ins
                vp = P.inc("tensor", vproj, "sPE", 1)

                P.wait("vector", "sPE", vp)

                def vev(E, p=p, wave=wave):
                    for tt in range(4):
                        t_c = wave * 4 + tt
                        ins = E.tensor_copy(
                            v_tok[:, t_c * 1024 + p * 256: t_c * 1024 + (p + 1) * 256],
                            PS4[tt])
                    return ins
                v_evac = P.inc("vector", vev, "sDVE", 1)
        pb_pe = P.cnt["sPE"]

        mark("pb")
        pa2_dve = emit_ln(x_own, xn_own, entry_pe=pb_pe)

        for t in ("xn_own", "xn_oth", "k_fm", "v_tok"):
            if t in tap_d:
                src = {"xn_own": xn_own, "xn_oth": bigx, "k_fm": k_fm,
                       "v_tok": v_tok}[t]
                P.wait("gpsimd", "sDVE", pa2_dve)
                P.wait("gpsimd", "sPE", P.cnt["sPE"])
                P.inc("gpsimd", lambda E, s=src, d=tap_d[t], w=tap_shapes[t][1]:
                      E.dma_start(out=d, in_=s[:, 0:w]), "sGp", 16)
        early_tap_gp = P.cnt["sGp"]
        if tap_d:
            P.wait("sync", "sGp", early_tap_gp)

        mark("pa2")
        attn_fm = bigx
        wqstrips = AR(STG, 16, BF16)
        s_sl = [AR(STG + 16, 4, F32), AR(STG + 20, 4, F32)]
        p_sl = [AR(STG + 24, 2, BF16), AR(STG + 26, 2, BF16)]
        pt_sl = [AR(STG + 28, 2, BF16), AR(STG + 30, 2, BF16)]
        qraw = AR(STG + 16, 4, BF16)
        qrot = AR(STG + 20, 4, BF16)
        qscr = AR(STG + 24, 2, BF16)
        psS = [psBf, psCf]
        psS_bf = [psB[:], psC[:]]
        PSQ = [psA[:, 0:256], psA[:, 512:768]]

        wq_3d = wq_scr.rearrange("(e p) o -> p e o", p=128)
        wqst_3d = wqstrips.rearrange("p (e o) -> p e o", o=256)

        def wqs(e_c):
            return wqstrips[:, e_c * 256:(e_c + 1) * 256]

        q_evac = 0
        strips_free = 0
        mx_, exp_, pn_, ptp_, ptc_, pv_, ae_ = {}, {}, {}, {}, {}, {}, {}
        ci = 0

        def emit_chain_head(c, hh, qi, kv, qfm_gate):
            N = (8 - qi) * 128
            sl = c % 2
            P.wait("tensor", "sDVE", max(qfm_gate, ptc_.get(c - 2, 0)))
            P.wait("tensor", "sACT", exp_.get(c - 2, 0))

            def scmm(E, hh=hh, qi=qi, kv=kv, N=N, sl=sl):
                q_t = q_pair[:, hh * 1024 + qi * 128: hh * 1024 + (qi + 1) * 128]
                for off, n in ((128, min(384, N - 128)), (512, N - 512)):
                    if n > 0:
                        ins = E.matmul(
                            psS[sl][:, off:off + n], q_t,
                            k_fm[:, kv * 1024 + qi * 128 + off:
                                 kv * 1024 + qi * 128 + off + n],
                            start=True, stop=True)
                ins = E.matmul(psS[sl][:, 0:128], q_t,
                               k_fm[:, kv * 1024 + qi * 128:
                                    kv * 1024 + (qi + 1) * 128],
                               start=True, stop=False)
                ins = E.matmul(psS[sl][:, 0:128], ident[:], mask[:],
                               start=False, stop=True)
                return ins
            return P.inc("tensor", scmm, "sPE", 1)

        def chain_B(c, qi, sc_done):
            N = (8 - qi) * 128
            sl = c % 2
            pS = p_sl[sl]
            mx = stats[:, 10 + sl:11 + sl]
            nmx = stats[:, 16 + sl:17 + sl]
            sm = stats[:, 12 + sl:13 + sl]

            P.wait("vector", "sPE", sc_done)
            t = P.inc("vector", lambda E, N=N: E.reduce_max(
                mx, psS[sl][:, 0:N], axis=AX.X), "sDVE", 1)
            P.wait("vector", "sDVE", t)
            mx_[c] = P.inc("vector", lambda E: E.tensor_scalar(
                nmx, mx, -1.0, None, ALU.mult), "sDVE", 1)

            P.wait("scalar", "sDVE", mx_[c])
            P.wait("scalar", "sPE", ptp_.get(c - 2, 0))
            exp_[c] = P.inc("scalar", lambda E, N=N: E.activation(
                pS[:, 0:N], psS[sl][:, 0:N], AF.Exp, bias=nmx, scale=1.0,
                accum_out=sm), "sACT", 1)

        def chain_C(c, qi):
            N = (8 - qi) * 128
            Ta = 8 - qi
            sl = c % 2
            pS = p_sl[sl]
            sm = stats[:, 12 + sl:13 + sl]
            rc = stats[:, 14 + sl:15 + sl]
            P.wait("vector", "sACT", exp_[c])
            t = P.inc("vector", lambda E: E.reciprocal(rc, sm), "sDVE", 1)
            P.wait("vector", "sDVE", t)
            pn_[c] = P.inc("vector", lambda E, N=N: E.tensor_scalar(
                pS[:, 0:N], pS[:, 0:N], rc, None, ALU.mult), "sDVE", 1)

            P.wait("tensor", "sDVE", pn_[c])

            def ptp(E, Ta=Ta, sl=sl):
                for j in range(Ta):
                    ins = E.transpose(psS_bf[sl][:, j * 128:(j + 1) * 128],
                                      pS[:, j * 128:(j + 1) * 128], ident[:])
                return ins
            ptp_[c] = P.inc("tensor", ptp, "sPE", 1)

        def chain_Cp(c, qi):
            N = (8 - qi) * 128
            sl = c % 2
            P.wait("vector", "sPE", max(ptp_[c], pv_.get(c - 2, 0)))
            ptc_[c] = P.inc("vector", lambda E, N=N, sl=sl: E.tensor_copy(
                pt_sl[sl][:, 0:N], psS_bf[sl][:, 0:N]), "sDVE", 1)

        def chain_E(c, qi, kv, head):
            Ta = 8 - qi
            sl = c % 2
            P.wait("tensor", "sDVE", ptc_[c])
            P.wait("tensor", "sACT", ae_.get(c - 1, 0))

            def pvmm(E, qi=qi, Ta=Ta, kv=kv, sl=sl):
                for j in range(Ta):
                    kj = qi + j
                    ins = E.matmul(
                        psD[:, 0:128],
                        v_tok[:, kj * 1024 + kv * 128: kj * 1024 + (kv + 1) * 128],
                        pt_sl[sl][:, j * 128:(j + 1) * 128],
                        start=(j == 0), stop=(j == Ta - 1))
                return ins
            pv_[c] = P.inc("tensor", pvmm, "sPE", 1)

            P.wait("scalar", "sPE", pv_[c])
            ae_[c] = P.inc("scalar", lambda E, head=head, qi=qi: E.activation(
                attn_fm[:, head * 1024 + qi * 128: head * 1024 + (qi + 1) * 128],
                psD[:, 0:128], AF.Copy, bias=0.0, scale=1.0), "sACT", 1)

        for PP in range(16):
            P.wait("sync", "sPq", prep["wq"])
            if PP == 0:
                P.wait("sync", "sDVE", pa2_dve)
                P.wait("sync", "sPE", pb_pe)
            P.wait("sync", "sPE", strips_free)
            for e0 in range(0, ECH, 8):
                P.inc("sync", lambda E, e0=e0, q=PP: E.dma_start(
                    out=wqst_3d[:, e0:e0 + 8, :],
                    in_=wq_3d[:, e0:e0 + 8, q * 256:(q + 1) * 256]), "sLoad", 16)
            sld = P.cnt["sLoad"]

            for wave in range(4):
                P.wait("tensor", "sLoad", sld)
                P.wait("tensor", "sDVE", max(q_evac, pa2_dve))

                def qproj(E, q=PP, wave=wave):
                    for e_c in range(ECH):
                        for tt in range(2):
                            ins = E.matmul(PSQ[tt],
                                           xn_tile(xn_own, e_c, wave * 2 + tt), wqs(e_c),
                                           start=(e_c == 0), stop=(e_c == ECH - 1))
                    return ins
                wdone = P.inc("tensor", qproj, "sPE", 1)

                P.wait("vector", "sPE", wdone)
                P.wait("vector", "sACT", ae_.get(ci - 1, 0))

                def qevac(E, wave=wave):
                    for tt in range(2):
                        t_c = wave * 2 + tt
                        ins = E.tensor_copy(qraw[:, t_c * 256:(t_c + 1) * 256],
                                            PSQ[tt])
                    return ins
                q_evac = P.inc("vector", qevac, "sDVE", 1)
            strips_free = P.cnt["sPE"]

            rope_done = emit_rope(qraw, qrot, qscr, cosq, sinq)

            for hh in range(2):
                P.wait("tensor", "sDVE", P.cnt["sDVE"])

                def qtp(E, hh=hh):
                    for t_c in range(8):
                        ins = E.transpose(psT[:, t_c * 128:(t_c + 1) * 128],
                                          qrot[:, t_c * 256 + hh * 128:
                                               t_c * 256 + (hh + 1) * 128],
                                          ident[:])
                    return ins
                tp = P.inc("tensor", qtp, "sPE", 1)
                P.wait("vector", "sPE", tp)
                P.inc("vector", lambda E, hh=hh: E.tensor_copy(
                    q_pair[:, hh * 1024:(hh + 1) * 1024], psT[:]), "sDVE", 1)
            qfm_done = P.cnt["sDVE"]

            if "q_fm" in tap_d and PP == 0:
                P.wait("gpsimd", "sDVE", qfm_done)
                P.inc("gpsimd", lambda E: E.dma_start(out=tap_d["q_fm"], in_=q_pair[:]),
                      "sGp", 16)

            chains = [(ci + 8 * hh + qi, hh, qi, (2 * PP + hh) // GQ,
                       2 * PP + hh) for hh in range(2) for qi in range(8)]
            sc_ = {}
            for i in range(len(chains) + 3):
                if 0 <= i - 2 < len(chains):
                    c, hh, qi, kv, head = chains[i - 2]
                    chain_C(c, qi)
                if 0 <= i - 1 < len(chains):
                    c, hh, qi, kv, head = chains[i - 1]
                    chain_B(c, qi, sc_[c])
                if 0 <= i - 2 < len(chains):
                    c, hh, qi, kv, head = chains[i - 2]
                    chain_Cp(c, qi)
                if i < len(chains):
                    c, hh, qi, kv, head = chains[i]
                    sc_[c] = emit_chain_head(c, hh, qi, kv, qfm_done)
                if 0 <= i - 3 < len(chains):
                    c, hh, qi, kv, head = chains[i - 3]
                    chain_E(c, qi, kv, head)
            ci += len(chains)
        attn_done = ae_[ci - 1]

        for t in ("attn_fm",):
            if t in tap_d:
                src = attn_fm
                P.wait("gpsimd", "sACT", attn_done)
                P.wait("gpsimd", "sDVE", P.cnt["sDVE"])
                P.inc("gpsimd", lambda E, s=src, d=tap_d[t], w=tap_shapes[t][1]:
                      E.dma_start(out=d, in_=s[:, 0:w]), "sGp", 16)

        mark("pc")
        wostrips = AR(128, 32, BF16)
        x_sb = [AR(STG, 2, F32), AR(STG + 2, 2, F32)]
        o_sb = [AR(STG + 4, 2, F32), AR(STG + 6, 2, F32)]
        PSW = [psA[:, 0:512], psA[:, 512:1024], psBf[:, 0:512], psBf[:, 512:1024]]

        def wos(o_c):
            return wostrips[:, o_c * 512:(o_c + 1) * 512]

        strips_free = 0
        wo_mm = {}
        wo_evac = {}
        wo_dma = {}
        for ef in range(8):
            P.wait("sync", "sPo", prep["wo"])
            if ef == 0:
                P.wait("sync", "sACT", attn_done)
                P.wait("sync", "sDVE", P.cnt["sDVE"])
            P.wait("sync", "sPE", strips_free)
            for o_c in range(ECH):
                P.inc("sync", lambda E, o=o_c, ef=ef: E.dma_start(
                    out=wos(o), in_=wo_scr[o * 128:(o + 1) * 128, ef * 512:(ef + 1) * 512]),
                    "sLoad", 16)
            sld = P.cnt["sLoad"]

            for t_c in range(8):
                k = ef * 8 + t_c
                xsem = "sX0" if t_c % 2 == 0 else "sX1"
                P.wait("sync", "sDVE", wo_evac.get(k - 2, 0))
                P.inc("sync", lambda E, t=t_c, ef=ef: E.dma_start(
                    out=x_sb[t % 2],
                    in_=x_own[t * 128:(t + 1) * 128, ef * 512:(ef + 1) * 512]),
                    xsem, 16)
                xld = (xsem, P.cnt[xsem])

                P.wait("tensor", "sLoad", sld)
                P.wait("tensor", "sDVE", wo_evac.get(k - 4, 0))

                def womm(E, ef=ef, t_c=t_c, k=k):
                    ps = PSW[k % 4]
                    for o_c in range(ECH):
                        ins = E.matmul(ps, attn_fm[:, o_c * 1024 + t_c * 128:
                                                   o_c * 1024 + (t_c + 1) * 128],
                                       wos(o_c), start=(o_c == 0), stop=(o_c == ECH - 1))
                    return ins
                wo_mm[k] = P.inc("tensor", womm, "sPE", 1)

                P.wait("vector", "sPE", wo_mm[k])
                P.wait("vector", xld[0], xld[1])
                P.wait("vector", "sGp", wo_dma.get(k - 1, 0))
                wo_evac[k] = P.inc("vector", lambda E, k=k, t=t_c: E.tensor_tensor(
                    o_sb[t % 2], PSW[k % 4], x_sb[t % 2], ALU.add), "sDVE", 1)

                P.wait("gpsimd", "sDVE", wo_evac[k])
                wo_dma[k] = P.inc("gpsimd", lambda E, t=t_c, ef=ef: E.dma_start(
                    out=out_d[t * 128:(t + 1) * 128, ef * 512:(ef + 1) * 512],
                    in_=o_sb[t % 2]), "sGp", 16)
            strips_free = P.cnt["sPE"]
        out_init_gp = P.cnt["sGp"]

        mark("pw")
        g16st = AR(64, 16, I16)
        u16st = AR(80, 16, I16)
        gbf = [AR(96, 16, BF16), AR(112, 16, BF16)]
        ubf = [AR(128, 16, BF16), AR(144, 16, BF16)]
        gate_sb = AR(STG, 2, BF16)
        up_sb = AR(STG + 2, 2, BF16)
        int_sb = [AR(STG + 4, 2, BF16), AR(STG + 6, 2, BF16)]

        gconv = {}
        uconv = {}
        gmm = {}
        umm = {}
        silu_done = {}
        upev = {}
        ttd = {}
        int_dma = {}
        pw_dve = P.cnt["sDVE"]
        g16_3d = g16.rearrange("(e p) i -> p e i", p=128)
        u16_3d = u16.rearrange("(e p) i -> p e i", p=128)
        g16st_3d = g16st[:].rearrange("p (e i) -> p e i", i=256)
        u16st_3d = u16st[:].rearrange("p (e i) -> p e i", i=256)
        for sb in range(NSB):
            slot = sb % 2
            P.wait("sync", "sDVE", gconv.get(sb - 1, 0))
            P.wait("gpsimd", "sACT", uconv.get(sb - 1, 0))
            if sb == 0:
                P.wait("sync", "sGp", out_init_gp)
                P.wait("sync", "sPE", P.cnt["sPE"])
                P.wait("gpsimd", "sPE", P.cnt["sPE"])
            for e0 in range(0, ECH, 8):
                P.inc("sync", lambda E, e0=e0, sb=sb: E.dma_start(
                    out=g16st_3d[:, e0:e0 + 8, :],
                    in_=g16_3d[:, e0:e0 + 8, sb * 256:(sb + 1) * 256]),
                    "sG", 16)
            gld = P.cnt["sG"]
            for e0 in range(0, ECH, 8):
                P.inc("gpsimd", lambda E, e0=e0, sb=sb: E.dma_start(
                    out=u16st_3d[:, e0:e0 + 8, :],
                    in_=u16_3d[:, e0:e0 + 8, sb * 256:(sb + 1) * 256]),
                    "sU", 16)
            uld = P.cnt["sU"]

            P.wait("vector", "sG", gld)
            P.wait("vector", "sPE", gmm.get(sb - 2, 0))
            gconv[sb] = P.inc("vector", lambda E, slot=slot: E.tensor_copy(
                gbf[slot][:], g16st[:]), "sDVE", 1)
            P.wait("scalar", "sU", uld)
            P.wait("scalar", "sPE", umm.get(sb - 2, 0))
            uconv[sb] = P.inc("scalar", lambda E, slot=slot: E.activation(
                ubf[slot][:], u16st[:], AF.Copy, bias=0.0, scale=1.0), "sACT", 1)

            for ic in range(2):
                icg = sb * 2 + ic
                psg = psA[:] if ic == 0 else psBf
                P.wait("tensor", "sDVE", max(gconv[sb], pw_dve))
                P.wait("tensor", "sACT", silu_done.get(icg - 2, 0))

                def gmmf(E, slot=slot, ic=ic, psg=psg):
                    for e_c in range(ECH):
                        for tg in range(2):
                            ins = E.matmul(psg[:, tg * 512:(tg + 1) * 512],
                                           gbf[slot][:, e_c * 256 + ic * 128:
                                                      e_c * 256 + (ic + 1) * 128],
                                           xn_half(xn_own, e_c, tg),
                                           start=(e_c == 0), stop=(e_c == ECH - 1))
                    return ins
                gmm[sb] = P.inc("tensor", gmmf, "sPE", 1)
                gmm_t = gmm[sb]
                P.wait("tensor", "sACT", uconv[sb])
                P.wait("tensor", "sDVE", upev.get(icg - 1, 0))

                def ummf(E, slot=slot, ic=ic):
                    for e_c in range(ECH):
                        for tg in range(2):
                            ins = E.matmul(psCf[:, tg * 512:(tg + 1) * 512],
                                           ubf[slot][:, e_c * 256 + ic * 128:
                                                      e_c * 256 + (ic + 1) * 128],
                                           xn_half(xn_own, e_c, tg),
                                           start=(e_c == 0), stop=(e_c == ECH - 1))
                    return ins
                umm[sb] = P.inc("tensor", ummf, "sPE", 1)
                umm_t = umm[sb]

                P.wait("scalar", "sPE", gmm_t)
                P.wait("scalar", "sDVE", ttd.get(icg - 1, 0))
                silu_done[icg] = P.inc("scalar", lambda E, icg=icg, psg=psg:
                                       E.activation(gate_sb[:], psg[:], AF.Silu,
                                                    bias=0.0,
                                                    scale=gs[:, icg:icg + 1]),
                                       "sACT", 1)
                P.wait("vector", "sPE", umm_t)
                P.wait("vector", "sDVE", ttd.get(icg - 1, 0))
                upev[icg] = P.inc("vector", lambda E, icg=icg: E.tensor_scalar(
                    up_sb[:], psCf, us[:, icg:icg + 1], None, ALU.mult), "sDVE", 1)
                P.wait("vector", "sACT", silu_done[icg])
                P.wait("vector", "sDVE", upev[icg])
                P.wait("vector", "sLoad", int_dma.get(icg - 1, 0))
                ttd[icg] = P.inc("vector", lambda E, icg=icg: E.tensor_tensor(
                    int_sb[icg % 2][:], gate_sb[:], up_sb[:], ALU.mult), "sDVE", 1)
                P.wait("sync", "sDVE", ttd[icg])
                int_dma[icg] = P.inc("sync", lambda E, icg=icg: E.dma_start(
                    out=inter_scr[icg * 128:(icg + 1) * 128, :],
                    in_=int_sb[icg % 2][:]), "sLoad", 16)
        pg_int_ld = P.cnt["sLoad"]
        pg_pe = P.cnt["sPE"]

        mark("pg")
        dstrips = [AR(0 + j * 8, 8, BF16) for j in range(IB_D)]
        d16stg = [AR(88, 8, I16), AR(96, 8, I16)]
        int_t = [AR(104 + 2 * j, 2, BF16) for j in range(IB_D)]
        dsb = AR(126, 16, F32)
        rows = [AR(142 + 16 * r, 16, F32) for r in range(3)]
        PS0 = [psA[:, 0:512], psA[:, 512:1024], psBf[:, 0:512], psBf[:, 512:1024]]
        PS1 = [psCf[:, 0:512], psCf[:, 512:1024], psD[:, 0:512], psTf]

        if paranoid >= 1:
            barrier()
        P.wait("sync", "sPE", pg_pe)
        P.wait("sync", "sACT", P.cnt["sACT"])
        P.wait("sync", "sDVE", P.cnt["sDVE"])
        P.wait("sync", "sLoad", pg_int_ld)
        P.inc("sync", lambda E: E.dma_start(out=dsb, in_=dsb_d), "sLoad", 16)
        dsb_ld = P.cnt["sLoad"]

        dconv = {}
        dload = {}
        iload = {}
        sup_mm = {}
        sup_ev = {}
        row_last = [0, 0, 0]
        blk_end_pe = {0: pg_pe}
        acc_total = {-1: out_init_gp}
        sup_idx = 0
        row_idx = 0
        i0 = 0
        for b, nb in enumerate(DOWN_BLOCKS):
            if paranoid >= 1:
                barrier()
            for j in range(nb):
                icg = i0 + j
                stg = j % 2
                P.wait("sync", "sDVE", dconv.get((b, j - 2), 0))
                if b > 0:
                    P.wait("sync", "sPE", blk_end_pe[b])
                dsem = "sD0" if stg == 0 else "sD1"
                P.inc("sync", lambda E, icg=icg, stg=stg: E.dma_start(
                    out=d16stg[stg][:], in_=d16[icg * 128:(icg + 1) * 128, :],
                    max_dma_last_dim=2048), dsem, 16)
                dload[(b, j)] = (dsem, P.cnt[dsem])
                P.inc("sync", lambda E, icg=icg, j=j: E.dma_start(
                    out=int_t[j][:], in_=inter_scr[icg * 128:(icg + 1) * 128, :]),
                    "sIl", 16)
                iload[(b, j)] = P.cnt["sIl"]

                P.wait("vector", dload[(b, j)][0], dload[(b, j)][1])
                if b > 0:
                    P.wait("vector", "sPE", blk_end_pe[b])
                dconv[(b, j)] = P.inc("vector", lambda E, j=j, stg=stg:
                                      E.tensor_copy(dstrips[j][:], d16stg[stg][:]),
                                      "sDVE", 1)

            if dbg_pd0 and b == 0:
                barrier()
                for j in range(nb):
                    P.inc("gpsimd", lambda G, j=j: G.dma_start(
                        out=pd0_d[j * 128:(j + 1) * 128, 0:E], in_=dstrips[j][:],
                        max_dma_last_dim=2048), "sGp", 16)
                    P.inc("gpsimd", lambda G, j=j: G.dma_start(
                        out=pd0_d[j * 128:(j + 1) * 128, E:E + TOK], in_=int_t[j][:]),
                        "sGp", 16)
                barrier()
            for tg in range(4):
                for hg in range(4):
                    if paranoid >= 2:
                        barrier()
                    ps_set = PS0 if sup_idx % 2 == 0 else PS1
                    P.wait("tensor", "sDVE", sup_ev.get(sup_idx - 2, 0))
                    if b == 0 and sup_idx == 0:
                        P.wait("tensor", "sPE", pg_pe)

                    def supmm(E, b=b, nb=nb, i0=i0, tg=tg, hg=hg, ps_set=ps_set):
                        for j in range(nb):
                            for u in range(4):
                                t_c = tg * 2 + u // 2
                                hf = hg * 2 + u % 2
                                ins = E.matmul(
                                    ps_set[u],
                                    int_t[j][:, t_c * 128:(t_c + 1) * 128],
                                    dstrips[j][:, hf * 512:(hf + 1) * 512],
                                    start=(j == 0), stop=(j == nb - 1))
                        return ins
                    P.wait("tensor", "sDVE", dconv[(b, nb - 1)])
                    P.wait("tensor", "sIl", iload[(b, nb - 1)])
                    sup_mm[sup_idx] = P.inc("tensor", supmm, "sPE", 1)

                    P.wait("vector", "sPE", sup_mm[sup_idx])
                    P.wait("vector", "sLoad", dsb_ld)
                    ra = row_idx % 3
                    rb = (row_idx + 1) % 3
                    if hg == 0:
                        P.wait("vector", "sR%d" % ra, row_last[ra])
                        P.wait("vector", "sR%d" % rb, row_last[rb])

                    def supev(E, tg=tg, hg=hg, ps_set=ps_set, ra=ra, rb=rb):
                        for u in range(4):
                            t_loc = u // 2
                            hf = hg * 2 + u % 2
                            r = rows[ra] if t_loc == 0 else rows[rb]
                            ins = E.tensor_tensor(
                                r[:, hf * 512:(hf + 1) * 512], ps_set[u],
                                dsb[:, hf * 512:(hf + 1) * 512], ALU.mult)
                        return ins
                    sup_ev[sup_idx] = P.inc("vector", supev, "sDVE", 1)
                    sup_idx += 1

                if paranoid >= 2:
                    barrier()
                P.wait("gpsimd", "sDVE", sup_ev[sup_idx - 1])
                if b == 0 and tg == 0:
                    P.wait("gpsimd", "sGp", acc_total[-1])
                for u in range(2):
                    t_c = tg * 2 + u
                    ri = (row_idx + u) % 3
                    P.inc("gpsimd", lambda E, t=t_c, r=rows[ri]: E.dma_start(
                        out=out_d[t * 128:(t + 1) * 128, :], in_=r[:],
                        accum_op=ALU.add, max_dma_last_dim=2048), "sR%d" % ri, 16)
                    if dbg_parts:
                        P.inc("gpsimd", lambda E, b=b, t=t_c, r=rows[ri]: E.dma_start(
                            out=parts_d[b * TOK + t * 128: b * TOK + (t + 1) * 128, :],
                            in_=r[:]), "sGp", 16)
                    row_last[ri] = P.cnt["sR%d" % ri]
                row_idx += 2
            blk_end_pe[b + 1] = P.cnt["sPE"]
            acc_total[b] = P.cnt["sGp"]
            i0 += nb

        P.wait("gpsimd", "sGp", P.cnt["sGp"])
        for ri in range(3):
            P.wait("gpsimd", "sR%d" % ri, P.cnt["sR%d" % ri])
        P.wait("gpsimd", "sLoad", P.cnt["sLoad"])

        mark("pd")
        if stop_phase is not None:
            m, cnts = phase_marks[stop_phase]
            for e in ENGS:
                P.ops[e] = P.ops[e][:m[e]]
            P.wait("sync", "sLoad", cnts["sLoad"])
            for s in ("sPrep", "sGp", "sPk", "sPv", "sPq", "sPo", "sG", "sU",
                      "sD0", "sD1", "sX0", "sX1", "sIl", "sR0", "sR1", "sR2"):
                P.wait("gpsimd", s, cnts[s])

        with nc.Block() as block:
            P.replay(block)

    return nc


def _rope_tok(positions):
    inv = (10000.0 ** (-(np.arange(0, D, 2, dtype=np.float32) / np.float32(D)))
           ).astype(np.float32)
    ang = positions.astype(np.float32)[:, None] * inv[None, :]
    cos = np.cos(ang).astype(bf).reshape(8, 128, 64).transpose(1, 0, 2).reshape(128, 512)
    sin = np.sin(ang).astype(bf).reshape(8, 128, 64).transpose(1, 0, 2).reshape(128, 512)
    return np.ascontiguousarray(cos), np.ascontiguousarray(sin)


def _i16(w):
    w = np.asarray(w)
    assert w.dtype == np.int32
    return np.ascontiguousarray(w.view(np.int16)[:, 0::2])


def make_in_maps(L0_attn_hidden_states, L0_attn_Wq, L0_attn_Wk, L0_attn_Wv,
                 L0_attn_Wo, L0_ffn_gate_wq, L0_ffn_up_wq, L0_ffn_down_wq,
                 L0_ffn_gate_scale, L0_ffn_up_scale, L0_ffn_down_scale):
    x = np.asarray(L0_attn_hidden_states, dtype=np.float32)
    shared = {
        "wq": np.ascontiguousarray(np.asarray(L0_attn_Wq, dtype=np.float32).T),
        "wk": np.ascontiguousarray(np.asarray(L0_attn_Wk, dtype=np.float32).T),
        "wv": np.ascontiguousarray(np.asarray(L0_attn_Wv, dtype=np.float32).T),
        "wo": np.ascontiguousarray(np.asarray(L0_attn_Wo, dtype=np.float32).T),
        "g16": np.ascontiguousarray(_i16(L0_ffn_gate_wq).T),
        "u16": np.ascontiguousarray(_i16(L0_ffn_up_wq).T),
        "d16": np.ascontiguousarray(_i16(L0_ffn_down_wq).T),
        "gs": np.ascontiguousarray(
            np.asarray(L0_ffn_gate_scale, dtype=np.float32).reshape(ICH, 128).T),
        "us": np.ascontiguousarray(
            np.asarray(L0_ffn_up_scale, dtype=np.float32).reshape(ICH, 128).T),
        "dsb": np.ascontiguousarray(np.broadcast_to(
            np.asarray(L0_ffn_down_scale, dtype=np.float32)[None, :], (128, E))),
        "mask": np.where(np.arange(128)[None, :] >= np.arange(128)[:, None],
                         np.float32(0.0), np.float32(NEG)).astype(bf),
        "ident": np.eye(128, dtype=np.float32).astype(bf),
    }
    in_maps = []
    for c in range(8):
        b, h = c // 2, c % 2
        xo = x[b, h * TOK:(h + 1) * TOK]
        xt = x[b, (1 - h) * TOK:(2 - h) * TOK]
        if h == 1:
            xo = xo[::-1]
            xt = xt[::-1]
            pos_q = 2047 - np.arange(TOK)
            pos_k = 1023 - np.arange(TOK)
        else:
            pos_q = np.arange(TOK)
            pos_k = 1024 + np.arange(TOK)
        cq, sq = _rope_tok(pos_q)
        ck, sk = _rope_tok(pos_k)
        m = dict(shared)
        m["x_own"] = np.ascontiguousarray(xo)
        m["x_oth"] = np.ascontiguousarray(xt)
        m["cosq"], m["sinq"], m["cosk"], m["sink"] = cq, sq, ck, sk
        in_maps.append(m)
    return in_maps


def kernel(**inputs):
    in_maps = make_in_maps(**inputs)
    nc = build_nc()
    res = run_bass_kernel_spmd(nc, in_maps, core_ids=list(range(8)))
    out = np.empty((B, S, E), dtype=np.float32)
    for c in range(8):
        b, h = c // 2, c % 2
        o = np.asarray(res.results[c]["out"])
        if h == 1:
            o = o[::-1]
        out[b, h * TOK:(h + 1) * TOK] = o
    return out
